# version 08
# baseline (speedup 1.0000x reference)
import math
import numpy as np
import concourse.bass as bass
import concourse.mybir as mybir
from concourse.bass_utils import run_bass_kernel_spmd

F32 = mybir.dt.float32
F32R = mybir.dt.float32r
BF16 = mybir.dt.bfloat16
AF = mybir.ActivationFunctionType
OP = mybir.AluOpType

NCORE = 8
T = 2048
C = 2048
CC = 16
NH = 16
NG = 4
HD = 128
NKT = 32
LN_EPS = 1e-5
QK_EPS = 1e-6
FFN_SCALE = 0.5 / math.sqrt(8.0)
QK_BIAS = -0.5 * math.log(128.0)

ENGINES = ("sync", "tensor", "scalar", "vector", "gpsimd")

DEBUG_TAPS = ()
QUADTREE = True


def R(ap):
    return ap.bitcast(F32R)


class FakeIns:
    def then_inc(self, *a, **k):
        return self


class Fake:
    def __getattr__(self, name):
        def f(*a, **k):
            return FakeIns()
        return f


class Sched:
    def __init__(self):
        self.counts = {}
        self.marks = {}
        self.frozen = False
        self.sems = {}
        self.slot_last = {}

    def reset(self):
        for k in list(self.counts):
            self.counts[k] = 0
        self.slot_last = {}
        self._last_tap = None

    def step(self, ins, sem, k=1, mark=None):
        v = self.counts.get(sem, 0) + k
        self.counts[sem] = v
        if not isinstance(ins, FakeIns) and self.sems.get(sem) is not None:
            ins.then_inc(self.sems[sem], k)
        if mark is not None:
            if not self.frozen:
                assert mark not in self.marks, f"dup mark {mark}"
                self.marks[mark] = (sem, v)
            else:
                assert self.marks.get(mark) == (sem, v), (
                    f"mark drift {mark}: {self.marks.get(mark)} != {(sem, v)}")
        return mark

    def wait(self, eng, mark):
        if mark is None:
            return
        sem, v = self.marks[mark]
        if not isinstance(eng, Fake):
            eng.wait_ge(self.sems[sem], v)

    def dma(self, e, ename, stream, out, in_, mark=None):
        ins = e[ename].dma_start(out=out, in_=in_)
        return self.step(ins, stream, 16, mark)

    def tap_dma(self, e, out, in_, mark):
        if getattr(self, "_last_tap", None):
            self.wait(e["sync"], self._last_tap)
        m = self.dma(e, "sync", "tapm", out, in_, mark)
        self._last_tap = m
        return m

    def rd(self, key, mark):
        if mark is not None:
            self.slot_last.setdefault(key, []).append(mark)

    def wr(self, eng, key):
        for m in self.slot_last.pop(key, []):
            self.wait(eng, m)


def mm512(e, S, ps, lhsT, rhs, start, stop, mark=None, ncol=None):
    te = e["tensor"]
    n = ncol if ncol is not None else rhs.shape[-1]
    ins = None
    for j in range(0, n, 512):
        w = min(512, n - j)
        ins = te.matmul(ps[:, j:j + w], lhsT, rhs[:, j:j + w],
                        start=start, stop=stop)
    return S.step(ins, "te", 1, mark)


def build_program(taps=()):
    nc = bass.Bass("TRN2", target_bir_lowering=False, dynamic_dma_scratch_size=8192,
                   detect_race_conditions=False)
    c = type("Ctx", (), {})()

    c.x_d = nc.dram_tensor("x", [C, T], F32, kind="ExternalInput").ap()
    c.wq_d = nc.dram_tensor("wq", [C, NH * HD], F32, kind="ExternalInput").ap()
    c.wk_d = nc.dram_tensor("wk", [C, NG * HD], F32, kind="ExternalInput").ap()
    c.wv_d = nc.dram_tensor("wv", [C, NG * HD], F32, kind="ExternalInput").ap()
    c.wo_d = nc.dram_tensor("wo", [NH * HD, C], F32, kind="ExternalInput").ap()
    c.fac_d = nc.dram_tensor("fac", [8, 256, C], F32, kind="ExternalInput").ap()
    c.bias_d = nc.dram_tensor("bias", [128, CC], F32, kind="ExternalInput").ap()
    c.out_d = nc.dram_tensor("out", [C, T], F32, kind="ExternalOutput").ap()

    c.wsum_in = nc.dram_tensor("wsum_in", [256, C], BF16, kind="Internal").ap()
    c.wsum_d = nc.dram_tensor("wsum", [C, C], BF16, kind="Internal").ap()
    c.wqb_d = nc.dram_tensor("wq_bf", [C, NH * HD], BF16, kind="Internal").ap()
    c.wkb_d = nc.dram_tensor("wk_bf", [C, NG * HD], BF16, kind="Internal").ap()
    c.wvb_d = nc.dram_tensor("wv_bf", [C, NG * HD], BF16, kind="Internal").ap()
    c.wob_d = nc.dram_tensor("wo_bf", [NH * HD, C], BF16, kind="Internal").ap()
    c.x1_d = nc.dram_tensor("x1d", [C, T], F32, kind="Internal").ap()
    c.x2_d = nc.dram_tensor("x2d", [C, T], F32, kind="Internal").ap()
    c.qh_d = nc.dram_tensor("qh_d", [NH, HD, T], F32R, kind="Internal").ap()
    c.kvk_in = nc.dram_tensor("kvk_in", [4, 128, T], F32R, kind="Internal").ap()
    c.kvk_out = nc.dram_tensor("kvk_out", [2, 4, 128, T], F32R, kind="Internal").ap()
    c.kvv_in = nc.dram_tensor("kvv_in", [4, 128, T], BF16, kind="Internal").ap()
    c.kvv_out = nc.dram_tensor("kvv_out", [2, 4, 128, T], BF16, kind="Internal").ap()
    c.bar_in = nc.dram_tensor("bar_in", [128, 4], F32, kind="Internal").ap()
    c.bar_out = nc.dram_tensor("bar_out", [128, 4], F32, kind="Internal").ap()
    c.bar2_out = nc.dram_tensor("bar2_out", [128, 4], F32, kind="Internal").ap()

    c.taps = {}
    for t_ in taps:
        shp, dt = {
            "t_wsum": ([C, C], BF16), "t_rsq1": ([128, T], F32),
            "t_xt": ([128, NH * T], BF16), "t_x1": ([C, T], F32),
            "t_qh": ([NH, HD, T], F32R), "t_kvin": ([4, 128, T], BF16),
            "t_kvout": ([2, 4, 128, T], BF16), "t_x2": ([C, T], F32),
            "t_osc": ([128, NH * T], BF16),
            "t_kh0": ([128, 2 * T], F32R), "t_vg0": ([128, 2 * T], BF16),
            "t_den0": ([128, T], F32R), "t_rcp0": ([128, T], F32),
        }[t_]
        c.taps[t_] = nc.dram_tensor(t_, shp, dt, kind="ExternalOutput").ap()

    def sb(name, shape, dt):
        return nc.alloc_sbuf_tensor(name, shape, dt)[:]

    c.xt_big = sb("xt_big", [128, NH * T], BF16)
    c.xin4 = sb("xin4", [128, 4 * T], F32)
    c.wb = sb("wb", [128, 2 * T], BF16)
    c.s_a = sb("s_a", [128, T], F32R)
    c.s_b = sb("s_b", [128, T], F32)
    c.khat = sb("khat", [128, 2 * T], F32R)
    c.qhat2 = sb("qhat2", [128, 2 * T], F32R)
    c.vg = sb("vg", [128, 2 * T], BF16)
    c.e3 = sb("e3", [128, 3 * T], BF16)
    c.pair = sb("pair", [128, T], BF16)
    c.xbf = sb("xbf", [128, 2 * T], BF16)
    c.qsq = sb("qsq", [128, T], BF16)
    c.arcp = sb("arcp", [128, T], F32)
    c.ones128 = sb("ones128", [128, 128], F32R)
    c.ones128b = sb("ones128b", [128, 128], BF16)
    c.onescol_b = sb("onescol_b", [128, 1], BF16)
    c.zero4 = sb("zero4", [128, 4], F32)
    c.eps5 = sb("eps5", [128, 1], F32)
    c.eps6 = sb("eps6", [128, 1], F32)
    c.qkb = sb("qkb", [128, 1], F32)
    c.bias_sb = sb("bias_sb", [128, CC], F32)
    c.halfb = sb("halfb", [128, CC], F32)
    c.wscol = sb("wscol", [1, C], BF16)
    c.wqcol = sb("wqcol", [1, NH * HD], BF16)
    c.wkcol = sb("wkcol", [1, NG * HD], BF16)
    c.wvcol = sb("wvcol", [1, NG * HD], BF16)
    c.shiftb = sb("shiftb", [1, T], BF16)

    c.psA = nc.alloc_psum_tensor("psA", [128, T], F32)[:]
    c.psB = nc.alloc_psum_tensor("psB", [128, T], F32)[:]

    c.xin = [c.xin4[:, i * T:(i + 1) * T] for i in range(4)]
    c.xt = lambda i: c.xt_big[:, i * T:(i + 1) * T]
    _wslots = [c.wb[:, 0:T], c.wb[:, T:2 * T]]
    c.wbs = None
    c.wbc = None
    c.e3s = [c.e3[:, i * T:(i + 1) * T] for i in range(3)]
    c.xbfs = [c.xbf[:, i * T:(i + 1) * T] for i in range(2)]
    c.wbs = _wslots + c.xbfs
    c.wbk = ("wb0", "wb1", "xbf0", "xbf1")
    c.wbc = lambda sl, cc2: c.wbs[sl][:, cc2 * 128:(cc2 + 1) * 128]
    c.e3view = c.e3.rearrange("p (g t d) -> p g t d", g=3, t=CC)
    c.ebs = [c.e3[:, 0:T], c.e3[:, T:2 * T], c.e3[:, 2 * T:3 * T], c.pair]
    c.ebk = ("eb0", "eb1", "eb2", "ebp")
    c.e6 = [c.e3[:, i * T:i * T + 1024] for i in range(3)]
    c.pairview = c.pair.rearrange("p (t d) -> p t d", t=CC)
    c.qh2 = [c.qhat2[:, i * T:(i + 1) * T] for i in range(2)]
    c.vstage = c.xin4.rearrange("p (g t d) -> p g t d", g=4, t=CC)

    S = Sched()

    def program(e):
        sy, te, se, ve, gp = (e[n] for n in ENGINES)

        S.dma(e, "sync", "bld", c.bias_sb, c.bias_d, "c.bld")
        ve.memset(c.ones128b, 1.0)
        ins = ve.memset(c.zero4, 0.0)
        S.step(ins, "ve", 1, "c.zero4")
        ve.memset(c.onescol_b, 1.0)
        ve.memset(c.eps5, LN_EPS)
        ve.memset(c.eps6, QK_EPS)
        ve.memset(c.qkb, QK_BIAS)
        S.wait(ve, "c.bld")
        ins = ve.tensor_scalar_mul(c.halfb, c.bias_sb, 0.5)
        S.step(ins, "ve", 1, "consts")
        S.wait(se, "consts")
        ins = se.activation(c.ones128, c.ones128b, AF.Copy)
        S.step(ins, "se", 1, "consts2")

        for hh in range(2):
            acc = c.qh2[hh]
            for rr in range(8):
                idx = hh * 8 + rr
                sl = idx % 4
                S.wr(sy, f"xin{sl}")
                m = S.dma(e, "sync", f"xin{sl}", c.xin[sl],
                          c.fac_d[rr, hh * 128:(hh + 1) * 128, :], f"p0.fld{idx}")
                S.wait(ve, m)
                if rr == 0:
                    ins = ve.tensor_copy(acc, c.xin[sl])
                else:
                    ins = ve.tensor_tensor(acc, acc, c.xin[sl], OP.add)
                mk = S.step(ins, "ve", 1, f"p0.fac{idx}")
                S.rd(f"xin{sl}", mk)
            S.wait(gp, f"p0.fac{hh * 8 + 7}")
            m = S.dma(e, "gpsimd", "ws", c.wsum_in[hh * 128:(hh + 1) * 128, :], acc,
                      f"p0.ws{hh}")
            S.rd(f"qh2{hh}", m)
        S.wait(gp, "c.zero4")
        S.dma(e, "gpsimd", "barw", c.bar_in, c.zero4, "p0.barw")
        S.wait(gp, "p0.ws0")
        S.wait(gp, "p0.ws1")
        S.wait(gp, "p0.barw")
        ins = gp.collective_compute(
            "AllGather", OP.bypass, ins=[c.wsum_in[:, :]], outs=[c.wsum_d[:, :]],
            replica_groups=[list(range(NCORE))])
        S.step(ins, "cc", 1, "cc.wsum0")
        ins = gp.collective_compute(
            "AllReduce", OP.add, ins=[c.bar_in[:, :]], outs=[c.bar_out[:, :]],
            replica_groups=[list(range(NCORE))])
        S.step(ins, "cc", 1, "cc.wsum")
        for nm, src, dst, nsl in (("wq", c.wq_d, c.wqb_d, 4), ("wk", c.wk_d, c.wkb_d, 1),
                                  ("wv", c.wv_d, c.wvb_d, 1), ("wo", c.wo_d, c.wob_d, 4)):
            rows = src.shape[0] // nsl
            for i in range(nsl):
                S.dma(e, "gpsimd", "wcast", dst[i * rows:(i + 1) * rows, :],
                      src[i * rows:(i + 1) * rows, :],
                      f"p0.{nm}bf{i}" if i == nsl - 1 else None)

        if "t_wsum" in c.taps:
            S.wait(sy, "cc.wsum")
            S.tap_dma(e, c.taps["t_wsum"][:, :], c.wsum_d[:, :], "tap.wsum")

        def ln_stats(tag, src_dram, dep=None):
            for cc2 in range(CC):
                sl = cc2 % 2
                xb, xbk = c.ebs[sl], c.ebk[sl]
                qb, qbk = c.ebs[2 + sl], c.ebk[2 + sl]
                S.wr(gp, xbk)
                if dep is not None:
                    S.wait(gp, dep(cc2))
                mld = S.dma(e, "gpsimd", f"lnld{sl}", xb,
                            src_dram[cc2 * 128:(cc2 + 1) * 128, :], f"{tag}.ld{cc2}")
                S.wait(se, mld)
                S.wr(se, qbk)
                ins = se.activation(qb, xb, AF.Square)
                msq_ = S.step(ins, "se", 1, f"{tag}.sq{cc2}")
                if cc2 == 0:
                    S.wait(te, "consts")
                    S.wr(te, "psA")
                    S.wr(te, "psB")
                S.wait(te, msq_)
                mm512(e, S, c.psA, c.ones128b, xb,
                      start=(cc2 == 0), stop=(cc2 == CC - 1))
                mk = mm512(e, S, c.psB, c.ones128b, qb,
                           start=(cc2 == 0), stop=(cc2 == CC - 1), mark=f"{tag}.mm{cc2}")
                S.rd(xbk, mk)
                S.rd(qbk, mk)
            S.wait(se, f"{tag}.mm{CC - 1}")
            S.wr(se, "s_a")
            ins = se.activation(c.s_a, c.psA, AF.Copy, scale=1.0 / C)
            mm_ = S.step(ins, "se", 1, f"{tag}.m")
            S.rd("psA", mm_)
            S.wait(ve, mm_)
            S.wr(ve, "s_b")
            ins = ve.tensor_tensor(c.s_b, c.s_a, c.s_a, OP.mult)
            S.step(ins, "ve", 1)
            ins = ve.scalar_tensor_tensor(c.s_b, c.psB, 1.0 / C, c.s_b,
                                          OP.mult, OP.subtract)
            mv = S.step(ins, "ve", 1, f"{tag}.var")
            S.rd("psB", mv)
            S.wait(se, mv)
            ins = se.activation(c.s_b, c.s_b, AF.Ln, bias=c.eps5)
            S.step(ins, "se", 1)
            ins = se.activation(c.s_b, c.s_b, AF.Exp, scale=-0.5)
            mr = S.step(ins, "se", 1, f"{tag}.rsq")
            S.wait(ve, mr)
            S.wr(ve, "shiftb")
            ins = ve.scalar_tensor_tensor(c.shiftb, c.s_a[0:1, :], -1.0,
                                          c.s_b[0:1, :], OP.mult, OP.mult)
            ms_ = S.step(ins, "ve", 1, f"{tag}.shift")
            S.rd("s_a", ms_)

        def ln_apply(tag, src_dram, statag, dep=None):
            for cc2 in range(CC):
                sl = cc2 % 4
                S.wr(sy, f"xin{sl}")
                if dep is not None:
                    S.wait(sy, dep(cc2))
                mld = S.dma(e, "sync", f"xin{sl}", c.xin[sl],
                            src_dram[cc2 * 128:(cc2 + 1) * 128, :], f"{tag}.ld{cc2}")
                S.wait(ve, mld)
                if cc2 == 0:
                    S.wait(ve, f"{statag}.rsq")
                    S.wr(ve, "xt")
                ins = ve.tensor_tensor(c.xt(cc2), c.xin[sl], c.s_b, OP.mult)
                mk = S.step(ins, "ve", 1, f"{tag}.xt{cc2}")
                S.rd(f"xin{sl}", mk)
                S.rd("s_b", mk)

        def ffn(tag, xdram, outdram, atag, statag, dep=None, spill_eng="scalar"):
            if tag == "f1":
                S.wait(sy, "cc.wsum")
            for et in range(CC):
                sl = et % 4
                S.wr(sy, c.wbk[sl])
                mwld = S.dma(e, "sync", c.wbk[sl], c.wbs[sl],
                             c.wsum_d[:, et * 128:(et + 1) * 128]
                             .rearrange("(cc p) d -> p cc d", p=128), f"{tag}.wld{et}")
                xsl = et % 2
                S.wr(sy, f"xin{xsl}")
                if dep is not None:
                    S.wait(sy, dep(et))
                mxld = S.dma(e, "sync", f"xin{xsl}", c.xin[xsl],
                             xdram[et * 128:(et + 1) * 128, :], f"{tag}.xld{et}")
                ps = c.psA if et % 2 == 0 else c.psB
                psk = "psA" if et % 2 == 0 else "psB"
                S.wait(te, mwld)
                if et == 0:
                    if tag == "f1":
                        S.wait(te, "f1cs.col")
                    for ccq in range(CC):
                        S.wait(te, f"{atag}.xt{ccq}")
                    S.wait(te, f"{statag}.shift")
                S.wr(te, psk)
                for ccq in range(CC):
                    mm512(e, S, ps, c.wbc(sl, ccq), c.xt(ccq),
                          start=(ccq == 0), stop=False)
                mk = mm512(e, S, ps, c.wscol[:, et * 128:(et + 1) * 128], c.shiftb,
                           start=False, stop=True, mark=f"{tag}.mm{et}")
                S.rd(c.wbk[sl], mk)
                if et == CC - 1:
                    S.rd("xt", mk)
                    S.rd("shiftb", mk)
                S.wait(se, mk)
                S.wr(se, f"xin{2 + xsl}")
                ins = se.activation(c.xin[2 + xsl], ps, AF.Identity, scale=FFN_SCALE,
                                    bias=c.halfb[:, et:et + 1])
                mev = S.step(ins, "se", 1, f"{tag}.ev{et}")
                S.rd(psk, mev)
                S.wait(ve, mev)
                S.wait(ve, mxld)
                ins = ve.tensor_tensor(c.xin[xsl], c.xin[xsl], c.xin[2 + xsl], OP.add)
                ma = S.step(ins, "ve", 1, f"{tag}.add{et}")
                S.rd(f"xin{2 + xsl}", ma)
                S.wait(e[spill_eng], ma)
                msp = S.dma(e, spill_eng, f"sp{xsl}",
                            outdram[et * 128:(et + 1) * 128, :],
                            c.xin[xsl], f"{tag}.sp{et}")
                S.rd(f"xin{xsl}", msp)

        def colsum(tag, wd, outcol, n2, dep):
            S.wait(sy, dep)
            for cc2 in range(CC):
                sl = cc2 % 4
                S.wr(sy, c.wbk[sl])
                mld = S.dma(e, "sync", c.wbk[sl], c.wbs[sl][:, 0:n2],
                            wd[cc2 * 128:(cc2 + 1) * 128, :], f"{tag}.ld{cc2}")
                if cc2 == 0:
                    S.wait(te, "consts")
                    S.wr(te, "psA")
                S.wait(te, mld)
                mk = mm512(e, S, c.psA[0:1, :], c.onescol_b, c.wbs[sl][:, 0:n2],
                           start=(cc2 == 0), stop=(cc2 == CC - 1),
                           mark=f"{tag}.mm{cc2}", ncol=n2)
                S.rd(c.wbk[sl], mk)
            S.wait(se, f"{tag}.mm{CC - 1}")
            ins = se.activation(outcol, c.psA[0:1, 0:n2], AF.Copy)
            mw = S.step(ins, "se", 1, f"{tag}.col")
            S.rd("psA", mw)

        ln_stats("l1s", c.x_d)
        colsum("f1cs", c.wsum_d, c.wscol, C, "cc.wsum")
        if "t_rsq1" in c.taps:
            S.wait(sy, "l1s.rsq")
            S.tap_dma(e, c.taps["t_rsq1"][:, :], c.s_b, "tap.rsq1")
            S.rd("s_b", "tap.rsq1")
        ln_apply("a1", c.x_d, "l1s")
        ffn("f1", c.x_d, c.x1_d, "a1", "l1s")
        if "t_x1" in c.taps:
            S.wait(sy, "f1.sp15")
            S.tap_dma(e, c.taps["t_x1"][:, :], c.x1_d[:, :], "tap.x1")

        colsum("csq", c.wqb_d, c.wqcol, C, "p0.wobf3")
        colsum("csk", c.wkb_d, c.wkcol, 512, "p0.wobf3")
        colsum("csv", c.wvb_d, c.wvcol, 512, "p0.wobf3")
        ln_stats("l2s", c.x1_d, dep=lambda cc2: f"f1.sp{cc2}")
        ln_apply("a2", c.x1_d, "l2s")
        if "t_xt" in c.taps:
            for cc2 in range(CC):
                S.wait(sy, f"a2.xt{cc2}")
            m = S.tap_dma(e, c.taps["t_xt"][:, :], c.xt_big, "tap.xt")
            S.rd("xt", m)

        for g in range(NG):
            sl = g % 4
            st_ = g % 2
            pp, ppk = (c.psA, "psA") if g % 2 == 0 else (c.psB, "psB")
            mp, mpk = (c.psB, "psB") if g % 2 == 0 else (c.psA, "psA")
            wcol, wcs = c.wkcol, slice(g * 128, (g + 1) * 128)
            S.wr(sy, c.wbk[sl])
            mwld = S.dma(e, "sync", c.wbk[sl], c.wbs[sl][:, 0:T],
                         c.wkb_d[:, g * 128:(g + 1) * 128]
                         .rearrange("(cc p) d -> p cc d", p=128), f"k.wld{g}")
            S.wait(te, mwld)
            if g == 0:
                S.wait(te, "csv.col")
                for ccq in range(CC):
                    S.wait(te, f"a2.xt{ccq}")
                for k_ in ("psA", "psB"):
                    S.wr(te, k_)
            mks = []
            for hs in range(2):
                hsl = slice(hs * 1024, (hs + 1) * 1024)
                S.wr(te, f"{ppk}{hs}")
                for ccq in range(CC):
                    mm512(e, S, pp[:, hsl], c.wbc(sl, ccq), c.xt(ccq)[:, hsl],
                          start=(ccq == 0), stop=False, ncol=1024)
                mk = mm512(e, S, pp[:, hsl], wcol[:, wcs],
                           c.shiftb[:, hsl], start=False, stop=True,
                           mark=f"k.mm{g}_{hs}", ncol=1024)
                mks.append(mk)
            S.rd(c.wbk[sl], mks[1])
            sqm = []
            for hs in range(2):
                hsl = slice(hs * 1024, (hs + 1) * 1024)
                S.wait(se, mks[1])
                S.wr(se, "qsq")
                ins = se.activation(c.qsq[:, hsl], pp[:, hsl], AF.Square)
                sqm.append(S.step(ins, "se", 1, f"k.sq{g}_{hs}"))
            mqm = []
            for hs in range(2):
                hsl = slice(hs * 1024, (hs + 1) * 1024)
                S.wait(te, sqm[hs])
                S.wr(te, f"{mpk}{hs}")
                mq = mm512(e, S, mp[:, hsl], c.ones128b, c.qsq[:, hsl],
                           start=True, stop=True, mark=f"k.msq{g}_{hs}",
                           ncol=1024)
                S.rd("qsq", mq)
                mqm.append(mq)
            mrs = []
            for hs in range(2):
                hsl = slice(hs * 1024, (hs + 1) * 1024)
                S.wait(se, mqm[1])
                S.wr(se, "s_b")
                ins = se.activation(c.s_b[:, hsl], mp[:, hsl], AF.Ln,
                                    scale=1.0 / 128, bias=c.eps6)
                ml = S.step(ins, "se", 1, f"k.ln{g}_{hs}")
                S.rd(f"{mpk}{hs}", ml)
                ins = se.activation(c.s_b[:, hsl], c.s_b[:, hsl], AF.Exp,
                                    scale=-0.5)
                mrs.append(S.step(ins, "se", 1, f"k.rsq{g}_{hs}"))
            mevs = []
            for hs in range(2):
                hsl = slice(hs * 1024, (hs + 1) * 1024)
                S.wait(ve, mrs[hs])
                if hs == 0:
                    S.wr(ve, f"qh2{st_}")
                ins = ve.tensor_tensor(c.qh2[st_][:, hsl], pp[:, hsl],
                                       c.s_b[:, hsl], OP.mult)
                mev = S.step(ins, "ve", 1, f"k.ev{g}_{hs}")
                S.rd(f"{ppk}{hs}", mev)
                S.rd("s_b", mev)
                mevs.append(mev)
            S.wait(gp, mevs[1])
            mki = S.dma(e, "gpsimd", "kvw", c.kvk_in[g, :, :], c.qh2[st_], f"k.in{g}")
            S.rd(f"qh2{st_}", mki)

        for g in range(NG):
            S.wait(gp, f"k.in{g}")
        ins = gp.collective_compute(
            "AllGather", OP.bypass, ins=[c.kvk_in[:, :, :]], outs=[c.kvk_out[:, :, :, :]],
            replica_groups=[[i, i + 1] for i in range(0, NCORE, 2)])
        S.step(ins, "cc", 1, "cc.k0")

        wvv = c.wvb_d.rearrange("(cg cc p) d -> cg p cc d", cg=4, p=128)
        for ttg in range(4):
            tts = [4 * ttg + i for i in range(4)]
            for ccg in range(4):
                sl = (ttg * 4 + ccg) % 4
                S.wr(sy, c.wbk[sl])
                mwv = S.dma(e, "sync", c.wbk[sl],
                            c.wbs[sl].rearrange("p (q d) -> p q d", q=4),
                            wvv[ccg, :, :, :], f"v.wld{ttg}_{ccg}")
                S.wait(te, mwv)
                if ttg == 0 and ccg == 0:
                    S.wait(te, "csv.col")
                    for k_ in ("psB", "psB0", "psB1"):
                        S.wr(te, k_)
                if ccg == 0:
                    for tt in tts:
                        if tt >= 4:
                            S.wait(te, f"v.ev{tt - 4}")
                mk = None
                for cc2 in range(ccg * 4, (ccg + 1) * 4):
                    for tt in tts:
                        vp = c.psB[:, (tt % 4) * 512:(tt % 4 + 1) * 512]
                        ins = te.matmul(vp, c.xt(cc2)[:, tt * 128:(tt + 1) * 128],
                                        c.wbs[sl][:, (cc2 % 4) * 512:(cc2 % 4 + 1) * 512],
                                        start=(cc2 == 0), stop=False)
                mk = S.step(ins, "te", 1, f"v.wcons{ttg}_{ccg}")
                S.rd(c.wbk[sl], mk)
            for tt in tts:
                vp = c.psB[:, (tt % 4) * 512:(tt % 4 + 1) * 512]
                ins = te.matmul(vp, c.shiftb[:, tt * 128:(tt + 1) * 128], c.wvcol,
                                start=False, stop=True)
                mk = S.step(ins, "te", 1, f"v.mm{tt}")
                S.wait(se, mk)
                if tt == 0:
                    for i in range(4):
                        S.wr(se, c.ebk[i])
                vpv = vp.rearrange("p (g d) -> p g d", g=4)
                ins = se.activation(c.e3view[:, :, tt, :], vpv[:, 0:3, :], AF.Copy)
                S.step(ins, "se", 1)
                ins = se.activation(c.pairview[:, tt, :], vpv[:, 3, :], AF.Copy)
                S.step(ins, "se", 1, f"v.ev{tt}")
        S.rd("psB", f"v.ev{CC - 1}")
        for g in range(NG):
            S.wait(gp, f"v.ev{CC - 1}")
            src = c.ebs[g]
            mvi = S.dma(e, "gpsimd", "kvw", c.kvv_in[g, :, :], src, f"v.in{g}")
            S.rd(c.ebk[g], mvi)
        if "t_kvin" in c.taps:
            S.tap_dma(e, c.taps["t_kvin"][:, :, :], c.kvv_in[:, :, :], "tap.kvin")
            S.wait(sy, "tap.kvin")
        for g in range(NG):
            S.wait(gp, f"v.in{g}")
        ins = gp.collective_compute(
            "AllGather", OP.bypass, ins=[c.kvv_in[:, :, :]], outs=[c.kvv_out[:, :, :, :]],
            replica_groups=[[i, i + 1] for i in range(0, NCORE, 2)])
        S.step(ins, "cc", 1, "cc.v0")
        ins = gp.collective_compute(
            "AllReduce", OP.add, ins=[c.bar_in[:, :]], outs=[c.bar2_out[:, :]],
            replica_groups=[[i, i + 1] for i in range(0, NCORE, 2)])
        S.step(ins, "cc", 1, "cc.kv")

        for h in range(NH):
            sl = h % 4
            st_ = h % 2
            pp, ppk = (c.psA, "psA") if h % 2 == 0 else (c.psB, "psB")
            mp, mpk = (c.psB, "psB") if h % 2 == 0 else (c.psA, "psA")
            S.wr(sy, c.wbk[sl])
            mwld = S.dma(e, "sync", c.wbk[sl], c.wbs[sl][:, 0:T],
                         c.wqb_d[:, h * 128:(h + 1) * 128]
                         .rearrange("(cc p) d -> p cc d", p=128), f"q.wld{h}")
            S.wait(te, mwld)
            if h == 0:
                S.wait(te, "csq.col")
            mks = []
            for hs in range(2):
                hsl = slice(hs * 1024, (hs + 1) * 1024)
                S.wr(te, f"{ppk}{hs}")
                for ccq in range(CC):
                    mm512(e, S, pp[:, hsl], c.wbc(sl, ccq), c.xt(ccq)[:, hsl],
                          start=(ccq == 0), stop=False, ncol=1024)
                mk = mm512(e, S, pp[:, hsl], c.wqcol[:, h * 128:(h + 1) * 128],
                           c.shiftb[:, hsl], start=False, stop=True,
                           mark=f"q.mm{h}_{hs}", ncol=1024)
                mks.append(mk)
            S.rd(c.wbk[sl], mks[1])
            if h == NH - 1:
                S.rd("xt", mks[1])
                S.rd("shiftb", mks[1])
            sqm = []
            for hs in range(2):
                hsl = slice(hs * 1024, (hs + 1) * 1024)
                S.wait(se, mks[1])
                S.wr(se, "qsq")
                ins = se.activation(c.qsq[:, hsl], pp[:, hsl], AF.Square)
                sqm.append(S.step(ins, "se", 1, f"q.sq{h}_{hs}"))
            mqm = []
            for hs in range(2):
                hsl = slice(hs * 1024, (hs + 1) * 1024)
                S.wait(te, sqm[hs])
                S.wr(te, f"{mpk}{hs}")
                mq = mm512(e, S, mp[:, hsl], c.ones128b, c.qsq[:, hsl],
                           start=True, stop=True, mark=f"q.msq{h}_{hs}",
                           ncol=1024)
                S.rd("qsq", mq)
                mqm.append(mq)
            mrs = []
            for hs in range(2):
                hsl = slice(hs * 1024, (hs + 1) * 1024)
                S.wait(se, mqm[1])
                S.wr(se, "s_b")
                ins = se.activation(c.s_b[:, hsl], mp[:, hsl], AF.Ln,
                                    scale=1.0 / 128, bias=c.eps6)
                ml = S.step(ins, "se", 1, f"q.ln{h}_{hs}")
                S.rd(f"{mpk}{hs}", ml)
                ins = se.activation(c.s_b[:, hsl], c.s_b[:, hsl], AF.Exp,
                                    scale=-0.5, bias=c.qkb)
                mrs.append(S.step(ins, "se", 1, f"q.rsq{h}_{hs}"))
            mevs = []
            for hs in range(2):
                hsl = slice(hs * 1024, (hs + 1) * 1024)
                S.wait(ve, mrs[hs])
                if hs == 0:
                    S.wr(ve, f"qh2{st_}")
                ins = ve.tensor_tensor(c.qh2[st_][:, hsl], pp[:, hsl],
                                       c.s_b[:, hsl], OP.mult)
                mev = S.step(ins, "ve", 1, f"q.ev{h}_{hs}")
                S.rd(f"{ppk}{hs}", mev)
                S.rd("s_b", mev)
                mevs.append(mev)
            S.wait(sy, mevs[1])
            msp = S.dma(e, "sync", f"qsp{st_}", c.qh_d[h, :, :], c.qh2[st_], f"q.sp{h}")
            S.rd(f"qh2{st_}", msp)
        if "t_qh" in c.taps:
            S.wait(sy, f"q.sp{NH - 1}")
            S.tap_dma(e, c.taps["t_qh"][:, :, :], c.qh_d[:, :, :], "tap.qh")
        if "t_kvout" in c.taps:
            S.wait(sy, "cc.kv")
            S.tap_dma(e, c.taps["t_kvout"][:, :, :, :], c.kvv_out[:, :, :, :],
                      "tap.kvout")

        for g in range(NG):
            S.wr(sy, "khat")
            if g == 0:
                S.wait(sy, "cc.kv")
            mk0 = S.dma(e, "sync", "khA", c.khat[:, 0:T], c.kvk_out[0, g, :, :],
                        f"at.k0{g}")
            mk1 = S.dma(e, "sync", "khB", c.khat[:, T:], c.kvk_out[1, g, :, :],
                        f"at.k1{g}")
            S.wr(sy, "vg")
            mv0 = S.dma(e, "sync", "vg0", c.vg[:, 0:T], c.kvv_out[0, g, :, :],
                        f"at.v0{g}")
            mv1 = S.dma(e, "sync", "vg1", c.vg[:, T:], c.kvv_out[1, g, :, :],
                        f"at.v1{g}")
            for h4 in range(4):
                h = 4 * g + h4
                sl = h % 2
                S.wr(sy, f"qh2{sl}")
                S.wait(sy, f"q.sp{h}")
                mq = S.dma(e, "sync", f"qld{sl}", c.qh2[sl], c.qh_d[h, :, :],
                           f"at.q{h}")
                if g == 0 and h4 == 0:
                    for k_ in ("psA", "psB"):
                        S.wr(te, k_)
                    S.wait(te, "consts2")
                for qs in range(2):
                    qsl = slice(qs * 1024, (qs + 1) * 1024)
                    ob = c.psB[:, qs * 1024:(qs + 1) * 1024]
                    for p in range(NKT // 2):
                        msc = None
                        for half in range(2):
                            kt = 2 * p + half
                            if kt == 0 and qs == 0:
                                S.wait(te, mq)
                                if h4 == 0:
                                    S.wait(te, mk0)
                                    S.wait(te, mk1)
                                    S.wait(te, mv0)
                                    S.wait(te, mv1)
                            S.wr(te, f"psA{half}")
                            msc = mm512(e, S, c.psA[:, half * 1024:(half + 1) * 1024],
                                        c.khat[:, kt * 128:(kt + 1) * 128],
                                        c.qh2[sl][:, qsl], start=True, stop=True,
                                        mark=f"at.sc{h}_{qs}_{kt}")
                            if kt == NKT - 1:
                                S.rd(f"qh2{sl}", msc)
                                if h4 == 3 and qs == 1:
                                    S.rd("khat", msc)
                        if p > 0:
                            for half in range(2):
                                kt1 = 2 * (p - 1) + half
                                if kt1 == 0:
                                    S.wr(te, f"psB{qs}")
                                mav = mm512(e, S, ob,
                                            c.vg[:, kt1 * 128:(kt1 + 1) * 128],
                                            c.ebs[(p - 1) % 3]
                                            [:, half * 1024:(half + 1) * 1024],
                                            start=(kt1 == 0), stop=False,
                                            mark=f"at.av{h}_{qs}_{kt1}", ncol=1024)
                                S.rd(c.ebk[(p - 1) % 3], mav)
                        S.wait(se, msc)
                        S.wr(se, c.ebk[p % 3])
                        ins = se.activation(c.ebs[p % 3], c.psA[:, 0:2048], AF.Exp)
                        mex = S.step(ins, "se", 1, f"at.ex{h}_{qs}_{p}")
                        S.rd("psA0", mex)
                        S.rd("psA1", mex)
                        S.wait(ve, mex)
                        if QUADTREE:
                            dst = c.pair[:, qsl] if p % 2 == 0 else c.qsq[:, qsl]
                            dk = "ebp" if p % 2 == 0 else "qsq"
                        else:
                            dst, dk = c.pair[:, qsl], "ebp"
                        S.wr(ve, dk)
                        ins = ve.tensor_tensor(dst, c.ebs[p % 3][:, 0:1024],
                                               c.ebs[p % 3][:, 1024:2048], OP.add)
                        mp = S.step(ins, "ve", 1, f"at.pr{h}_{qs}_{p}")
                        S.rd(c.ebk[p % 3], mp)
                        if QUADTREE and p % 2 == 1:
                            ins = ve.tensor_tensor(c.qsq[:, qsl], c.pair[:, qsl],
                                                   c.qsq[:, qsl], OP.add)
                            mp2 = S.step(ins, "ve", 1, f"at.qd{h}_{qs}_{p}")
                            S.rd("ebp", mp2)
                            if p == 1:
                                S.wr(ve, f"s_a{qs}")
                                ins = ve.tensor_copy(c.s_a[:, qsl], c.qsq[:, qsl])
                            else:
                                ins = ve.tensor_tensor(c.s_a[:, qsl], c.s_a[:, qsl],
                                                       c.qsq[:, qsl], OP.add)
                            mac = S.step(ins, "ve", 1, f"at.ac{h}_{qs}_{2 * p + 1}")
                            S.rd("qsq", mac)
                        elif not QUADTREE:
                            if p == 0:
                                S.wr(ve, f"s_a{qs}")
                                ins = ve.tensor_copy(c.s_a[:, qsl], c.pair[:, qsl])
                            else:
                                ins = ve.tensor_tensor(c.s_a[:, qsl], c.s_a[:, qsl],
                                                       c.pair[:, qsl], OP.add)
                            mac = S.step(ins, "ve", 1, f"at.ac{h}_{qs}_{2 * p + 1}")
                            S.rd("ebp", mac)
                    S.wait(te, f"at.ex{h}_{qs}_{NKT // 2 - 1}")
                    mav = None
                    for half in range(2):
                        kt1 = NKT - 2 + half
                        mav = mm512(e, S, ob, c.vg[:, kt1 * 128:(kt1 + 1) * 128],
                                    c.ebs[(NKT // 2 - 1) % 3]
                                    [:, half * 1024:(half + 1) * 1024],
                                    start=False, stop=(half == 1),
                                    mark=f"at.av{h}_{qs}_{kt1}", ncol=1024)
                    S.rd(c.ebk[(NKT // 2 - 1) % 3], mav)
                    if h4 == 3 and qs == 1:
                        S.rd("vg", mav)
                    S.wait(te, f"at.ac{h}_{qs}_{NKT - 1}")
                    S.wr(te, "psA1")
                    md = mm512(e, S, c.psA[:, 1024:2048], c.ones128, c.s_a[:, qsl],
                               start=True, stop=True, mark=f"at.ds{h}_{qs}",
                               ncol=1024)
                    S.rd(f"s_a{qs}", md)
                    S.wait(se, md)
                    S.wr(se, "s_b")
                    ins = se.activation(c.s_b[:, 0:1024], c.psA[:, 1024:2048], AF.Ln)
                    ml_ = S.step(ins, "se", 1, f"at.dln{h}_{qs}")
                    S.rd("psA1", ml_)
                    S.wr(se, f"arcp{qs}")
                    ins = se.activation(c.arcp[:, qsl], c.s_b[:, 0:1024], AF.Exp,
                                        scale=-1.0)
                    mrc = S.step(ins, "se", 1, f"at.rc{h}_{qs}")
                    S.rd("s_b", mrc)
                    S.wait(ve, mav)
                    S.wait(ve, mrc)
                    if h == 0 and qs == 0:
                        S.wr(ve, "xt")
                    ins = ve.tensor_tensor(c.xt(h)[:, qsl], ob, c.arcp[:, qsl],
                                           OP.mult)
                    mo = S.step(ins, "ve", 1, f"at.o{h}_{qs}")
                    S.rd(f"psB{qs}", mo)
                    S.rd(f"arcp{qs}", mo)
        if "t_osc" in c.taps:
            for h in range(NH):
                S.wait(sy, f"at.o{h}_0")
                S.wait(sy, f"at.o{h}_1")
            m = S.tap_dma(e, c.taps["t_osc"][:, :], c.xt_big, "tap.osc")
            S.rd("xt", m)

        S.wait(sy, "p0.wobf3")
        for ct in range(CC):
            sl = ct % 4
            xsl = ct % 2
            S.wr(sy, c.wbk[sl])
            mwld = S.dma(e, "sync", c.wbk[sl], c.wbs[sl],
                         c.wob_d[:, ct * 128:(ct + 1) * 128]
                         .rearrange("(hh p) d -> p hh d", p=128), f"wo.wld{ct}")
            S.wr(sy, f"xin{xsl}")
            S.wait(sy, f"f1.sp{ct}")
            mxld = S.dma(e, "sync", f"xin{xsl}", c.xin[xsl],
                         c.x1_d[ct * 128:(ct + 1) * 128, :], f"wo.xld{ct}")
            ps = c.psA if ct % 2 == 0 else c.psB
            psk = "psA" if ct % 2 == 0 else "psB"
            S.wait(te, mwld)
            if ct == 0:
                for k_ in ("psA0", "psA1", "psB0", "psB1"):
                    S.wr(te, k_)
                for h in range(NH):
                    S.wait(te, f"at.o{h}_0")
                    S.wait(te, f"at.o{h}_1")
            S.wr(te, psk)
            mk = None
            for h in range(NH):
                mk = mm512(e, S, ps, c.wbc(sl, h), c.xt(h),
                           start=(h == 0), stop=(h == NH - 1),
                           mark=f"wo.mm{ct}" if h == NH - 1 else None)
            S.rd(c.wbk[sl], mk)
            if ct == CC - 1:
                S.rd("xt", mk)
            S.wait(ve, mk)
            S.wait(ve, mxld)
            ins = ve.scalar_tensor_tensor(c.xin[xsl], ps, 1.0, c.xin[xsl],
                                          OP.mult, OP.add)
            ma = S.step(ins, "ve", 1, f"wo.add{ct}")
            S.rd(psk, ma)
            S.wait(se, ma)
            msp = S.dma(e, "scalar", f"sp{xsl}", c.x2_d[ct * 128:(ct + 1) * 128, :],
                        c.xin[xsl], f"wo.sp{ct}")
            S.rd(f"xin{xsl}", msp)
        if "t_x2" in c.taps:
            S.wait(sy, f"wo.sp{CC - 1}")
            S.tap_dma(e, c.taps["t_x2"][:, :], c.x2_d[:, :], "tap.x2")

        ln_stats("l3s", c.x2_d, dep=lambda cc2: f"wo.sp{cc2}")
        ln_apply("a3", c.x2_d, "l3s")
        ffn("f3", c.x2_d, c.out_d, "a3", "l3s")

        totals = dict(S.counts)
        for en in ("tensor", "scalar", "vector", "gpsimd"):
            ins = e[en].nop()
            S.step(ins, "fin", 1)
        if not isinstance(sy, Fake):
            for name in sorted(totals):
                if totals[name] > 0:
                    sy.wait_ge(S.sems[name], totals[name])
            sy.wait_ge(S.sems["fin"], 4)
            for name in sorted(S.counts):
                sy.sem_clear(S.sems[name])

    fake = {n: Fake() for n in ENGINES}
    program(fake)
    S.frozen = True
    final_counts = dict(S.counts)

    from contextlib import ExitStack
    with ExitStack() as stack:
        sems = {}
        for name in sorted(final_counts):
            sems[name] = stack.enter_context(nc.semaphore(name))
        block = stack.enter_context(nc.Block())
        S.sems = sems

        def section(name):
            def f(eng):
                S.reset()
                program({**fake, name: eng})
                assert S.counts == final_counts, (name, S.counts, final_counts)
            return f

        block.sync(section("sync"))
        block.tensor(section("tensor"))
        block.scalar(section("scalar"))
        block.vector(section("vector"))
        block.gpsimd(section("gpsimd"))

    return nc


_CACHE = {}


def _get_nc():
    if "nc" not in _CACHE:
        _CACHE["nc"] = build_program(DEBUG_TAPS)
    return _CACHE["nc"]


def kernel(**inputs):
    x = np.asarray(inputs["L0_attn_x_q"], dtype=np.float32)
    wq = np.asarray(inputs["L0_attn_Wq"], dtype=np.float32).reshape(C, NH * HD)
    wk = np.asarray(inputs["L0_attn_Wk"], dtype=np.float32).reshape(C, NG * HD)
    wv = np.asarray(inputs["L0_attn_Wv"], dtype=np.float32).reshape(C, NG * HD)
    wo = np.ascontiguousarray(np.asarray(inputs["L0_attn_Wo"], dtype=np.float32))
    fac = np.asarray(inputs["L0_ffn_factors"], dtype=np.float32)
    bias = np.ascontiguousarray(
        np.asarray(inputs["L0_ffn_bias"], dtype=np.float32).reshape(CC, 128).T)

    xf = x.reshape(NCORE * T, C)
    nc = _get_nc()
    in_maps = []
    for cid in range(NCORE):
        in_maps.append({
            "x": np.ascontiguousarray(xf[cid * T:(cid + 1) * T].T),
            "wq": wq, "wk": wk, "wv": wv, "wo": wo,
            "fac": np.ascontiguousarray(fac[:, cid * 256:(cid + 1) * 256, :]),
            "bias": bias,
        })
    res = run_bass_kernel_spmd(nc, in_maps, core_ids=list(range(NCORE)))
    out = np.empty((NCORE * T, C), dtype=np.float32)
    for cid in range(NCORE):
        out[cid * T:(cid + 1) * T] = res.results[cid]["out"].T
    return out.reshape(4, 4096, C)


# version 10 (final)
# speedup vs baseline: 1.1876x; 1.1876x over previous
import math
import numpy as np
import concourse.bass as bass
import concourse.mybir as mybir
from concourse.bass_utils import run_bass_kernel_spmd

F32 = mybir.dt.float32
F32R = mybir.dt.float32r
BF16 = mybir.dt.bfloat16
AF = mybir.ActivationFunctionType
OP = mybir.AluOpType

NCORE = 8
T = 2048
C = 2048
CC = 16
NH = 16
NG = 4
HD = 128
NKT = 32
LN_EPS = 1e-5
QK_EPS = 1e-6
FFN_SCALE = 0.5 / math.sqrt(8.0)
QK_BIAS = -0.5 * math.log(128.0)

ENGINES = ("sync", "tensor", "scalar", "vector", "gpsimd")

DEBUG_TAPS = ()
QUADTREE = True


def R(ap):
    return ap.bitcast(F32R)


class FakeIns:
    def then_inc(self, *a, **k):
        return self


class Fake:
    def __getattr__(self, name):
        def f(*a, **k):
            return FakeIns()
        return f


class Sched:
    def __init__(self):
        self.counts = {}
        self.marks = {}
        self.frozen = False
        self.sems = {}
        self.slot_last = {}

    def reset(self):
        for k in list(self.counts):
            self.counts[k] = 0
        self.slot_last = {}
        self._last_tap = None
        self.wm = {}

    def step(self, ins, sem, k=1, mark=None):
        v = self.counts.get(sem, 0) + k
        self.counts[sem] = v
        if not isinstance(ins, FakeIns) and self.sems.get(sem) is not None:
            ins.then_inc(self.sems[sem], k)
            if k == 1:
                key = (id(ins), )
        if mark is not None:
            if not self.frozen:
                assert mark not in self.marks, f"dup mark {mark}"
                self.marks[mark] = (sem, v)
            else:
                assert self.marks.get(mark) == (sem, v), (
                    f"mark drift {mark}: {self.marks.get(mark)} != {(sem, v)}")
        return mark

    def wait(self, eng, mark):
        if mark is None:
            return
        sem, v = self.marks[mark]
        if not isinstance(eng, Fake):
            key = (id(eng), sem)
            if self.wm.get(key, -1) >= v:
                return
            self.wm[key] = v
            eng.wait_ge(self.sems[sem], v)

    def dma(self, e, ename, stream, out, in_, mark=None):
        ins = e[ename].dma_start(out=out, in_=in_)
        return self.step(ins, stream, 16, mark)

    def tap_dma(self, e, out, in_, mark):
        if getattr(self, "_last_tap", None):
            self.wait(e["sync"], self._last_tap)
        m = self.dma(e, "sync", "tapm", out, in_, mark)
        self._last_tap = m
        return m

    def rd(self, key, mark):
        if mark is not None:
            self.slot_last.setdefault(key, []).append(mark)

    def wr(self, eng, key):
        for m in self.slot_last.pop(key, []):
            self.wait(eng, m)


def mm512(e, S, ps, lhsT, rhs, start, stop, mark=None, ncol=None):
    te = e["tensor"]
    n = ncol if ncol is not None else rhs.shape[-1]
    ins = None
    for j in range(0, n, 512):
        w = min(512, n - j)
        ins = te.matmul(ps[:, j:j + w], lhsT, rhs[:, j:j + w],
                        start=start, stop=stop)
    return S.step(ins, "te", 1, mark)


def build_program(taps=()):
    nc = bass.Bass("TRN2", target_bir_lowering=False, dynamic_dma_scratch_size=8192,
                   detect_race_conditions=False)
    c = type("Ctx", (), {})()

    c.x_d = nc.dram_tensor("x", [C, T], F32, kind="ExternalInput").ap()
    c.wq_d = nc.dram_tensor("wq", [C, NH * HD], F32, kind="ExternalInput").ap()
    c.wk_d = nc.dram_tensor("wk", [C, NG * HD], F32, kind="ExternalInput").ap()
    c.wv_d = nc.dram_tensor("wv", [C, NG * HD], F32, kind="ExternalInput").ap()
    c.wo_d = nc.dram_tensor("wo", [NH * HD, C], F32, kind="ExternalInput").ap()
    c.fac_d = nc.dram_tensor("fac", [8, 256, C], F32, kind="ExternalInput").ap()
    c.bias_d = nc.dram_tensor("bias", [128, CC], F32, kind="ExternalInput").ap()
    c.out_d = nc.dram_tensor("out", [C, T], F32, kind="ExternalOutput").ap()

    c.wsum_in = nc.dram_tensor("wsum_in", [256, C], BF16, kind="Internal").ap()
    c.wsum_d = nc.dram_tensor("wsum", [C, C], BF16, kind="Internal").ap()
    c.wqb_d = nc.dram_tensor("wq_bf", [C, NH * HD], BF16, kind="Internal").ap()
    c.wkb_d = nc.dram_tensor("wk_bf", [C, NG * HD], BF16, kind="Internal").ap()
    c.wvb_d = nc.dram_tensor("wv_bf", [C, NG * HD], BF16, kind="Internal").ap()
    c.wob_d = nc.dram_tensor("wo_bf", [NH * HD, C], BF16, kind="Internal").ap()
    c.x1_d = nc.dram_tensor("x1d", [C, T], F32, kind="Internal").ap()
    c.x2_d = nc.dram_tensor("x2d", [C, T], F32, kind="Internal").ap()
    c.qh_d = nc.dram_tensor("qh_d", [NH, HD, T], F32R, kind="Internal").ap()
    c.kvk_in = nc.dram_tensor("kvk_in", [4, 128, T], F32R, kind="Internal").ap()
    c.kvk_out = nc.dram_tensor("kvk_out", [2, 4, 128, T], F32R, kind="Internal").ap()
    c.kvv_in = nc.dram_tensor("kvv_in", [4, 128, T], BF16, kind="Internal").ap()
    c.kvv_out = nc.dram_tensor("kvv_out", [2, 4, 128, T], BF16, kind="Internal").ap()
    c.bar_in = nc.dram_tensor("bar_in", [128, 4], F32, kind="Internal").ap()
    c.bar_out = nc.dram_tensor("bar_out", [128, 4], F32, kind="Internal").ap()
    c.bar2_out = nc.dram_tensor("bar2_out", [128, 4], F32, kind="Internal").ap()

    c.taps = {}
    for t_ in taps:
        shp, dt = {
            "t_wsum": ([C, C], BF16), "t_rsq1": ([128, T], F32),
            "t_xt": ([128, NH * T], BF16), "t_x1": ([C, T], F32),
            "t_qh": ([NH, HD, T], F32R), "t_kvin": ([4, 128, T], BF16),
            "t_kvout": ([2, 4, 128, T], BF16), "t_x2": ([C, T], F32),
            "t_osc": ([128, NH * T], BF16),
            "t_kh0": ([128, 2 * T], F32R), "t_vg0": ([128, 2 * T], BF16),
            "t_den0": ([128, T], F32R), "t_rcp0": ([128, T], F32),
        }[t_]
        c.taps[t_] = nc.dram_tensor(t_, shp, dt, kind="ExternalOutput").ap()

    def sb(name, shape, dt):
        return nc.alloc_sbuf_tensor(name, shape, dt)[:]

    c.xt_big = sb("xt_big", [128, NH * T], BF16)
    c.xin4 = sb("xin4", [128, 4 * T], F32)
    c.wb = sb("wb", [128, 2 * T], BF16)
    c.s_a = sb("s_a", [128, T], F32R)
    c.s_b = sb("s_b", [128, T], F32)
    c.khat = sb("khat", [128, 2 * T], F32R)
    c.qhat2 = sb("qhat2", [128, 2 * T], F32R)
    c.vg = sb("vg", [128, 2 * T], BF16)
    c.e3 = sb("e3", [128, 3 * T], BF16)
    c.pair = sb("pair", [128, T], BF16)
    c.xbf = sb("xbf", [128, 2 * T], BF16)
    c.qsq = sb("qsq", [128, T], BF16)
    c.arcp = sb("arcp", [128, T], F32)
    c.ones128 = sb("ones128", [128, 128], F32R)
    c.ones128b = sb("ones128b", [128, 128], BF16)
    c.onescol_b = sb("onescol_b", [128, 1], BF16)
    c.zero4 = sb("zero4", [128, 4], F32)
    c.eps5 = sb("eps5", [128, 1], F32)
    c.eps6 = sb("eps6", [128, 1], F32)
    c.qkb = sb("qkb", [128, 1], F32)
    c.bias_sb = sb("bias_sb", [128, CC], F32)
    c.halfb = sb("halfb", [128, CC], F32)
    c.wscol = sb("wscol", [1, C], BF16)
    c.wqcol = sb("wqcol", [1, NH * HD], BF16)
    c.wkcol = sb("wkcol", [1, NG * HD], BF16)
    c.wvcol = sb("wvcol", [1, NG * HD], BF16)
    c.shiftb = sb("shiftb", [1, T], BF16)

    c.psA = nc.alloc_psum_tensor("psA", [128, T], F32)[:]
    c.psB = nc.alloc_psum_tensor("psB", [128, T], F32)[:]

    c.xin = [c.xin4[:, i * T:(i + 1) * T] for i in range(4)]
    c.xt = lambda i: c.xt_big[:, i * T:(i + 1) * T]
    _wslots = [c.wb[:, 0:T], c.wb[:, T:2 * T]]
    c.wbs = None
    c.wbc = None
    c.e3s = [c.e3[:, i * T:(i + 1) * T] for i in range(3)]
    c.xbfs = [c.xbf[:, i * T:(i + 1) * T] for i in range(2)]
    c.wbs = _wslots + c.xbfs
    c.wbk = ("wb0", "wb1", "xbf0", "xbf1")
    c.wbc = lambda sl, cc2: c.wbs[sl][:, cc2 * 128:(cc2 + 1) * 128]
    c.e3view = c.e3.rearrange("p (g t d) -> p g t d", g=3, t=CC)
    c.ebs = [c.e3[:, 0:T], c.e3[:, T:2 * T], c.e3[:, 2 * T:3 * T], c.pair]
    c.ebk = ("eb0", "eb1", "eb2", "ebp")
    c.e6 = [c.e3[:, i * T:i * T + 1024] for i in range(3)]
    c.pairview = c.pair.rearrange("p (t d) -> p t d", t=CC)
    c.qh2 = [c.qhat2[:, i * T:(i + 1) * T] for i in range(2)]
    c.vstage = c.xin4.rearrange("p (g t d) -> p g t d", g=4, t=CC)

    S = Sched()

    def program(e):
        sy, te, se, ve, gp = (e[n] for n in ENGINES)

        S.dma(e, "sync", "bld", c.bias_sb, c.bias_d, "c.bld")
        ve.memset(c.ones128b, 1.0)
        ins = ve.memset(c.zero4, 0.0)
        S.step(ins, "ve", 1, "c.zero4")
        ve.memset(c.onescol_b, 1.0)
        ve.memset(c.eps5, LN_EPS)
        ve.memset(c.eps6, QK_EPS)
        ve.memset(c.qkb, QK_BIAS)
        S.wait(ve, "c.bld")
        ins = ve.tensor_scalar_mul(c.halfb, c.bias_sb, 0.5)
        S.step(ins, "ve", 1, "consts")
        S.wait(se, "consts")
        ins = se.activation(c.ones128, c.ones128b, AF.Copy)
        S.step(ins, "se", 1, "consts2")

        for hh in range(2):
            acc = c.qh2[hh]
            for rr in range(8):
                idx = hh * 8 + rr
                sl = idx % 4
                S.wr(sy, f"xin{sl}")
                m = S.dma(e, "sync", f"xin{sl}", c.xin[sl],
                          c.fac_d[rr, hh * 128:(hh + 1) * 128, :], f"p0.fld{idx}")
                S.wait(ve, m)
                if rr == 0:
                    ins = ve.tensor_copy(acc, c.xin[sl])
                else:
                    ins = ve.tensor_tensor(acc, acc, c.xin[sl], OP.add)
                mk = S.step(ins, "ve", 1, f"p0.fac{idx}")
                S.rd(f"xin{sl}", mk)
            S.wait(gp, f"p0.fac{hh * 8 + 7}")
            m = S.dma(e, "gpsimd", "ws", c.wsum_in[hh * 128:(hh + 1) * 128, :], acc,
                      f"p0.ws{hh}")
            S.rd(f"qh2{hh}", m)
        S.wait(gp, "c.zero4")
        S.dma(e, "gpsimd", "barw", c.bar_in, c.zero4, "p0.barw")
        S.wait(gp, "p0.ws0")
        S.wait(gp, "p0.ws1")
        S.wait(gp, "p0.barw")
        ins = gp.collective_compute(
            "AllGather", OP.bypass, ins=[c.wsum_in[:, :]], outs=[c.wsum_d[:, :]],
            replica_groups=[list(range(NCORE))])
        S.step(ins, "cc", 1, "cc.wsum0")
        ins = gp.collective_compute(
            "AllReduce", OP.add, ins=[c.bar_in[:, :]], outs=[c.bar_out[:, :]],
            replica_groups=[list(range(NCORE))])
        S.step(ins, "cc", 1, "cc.wsum")
        for nm, src, dst, nsl in (("wq", c.wq_d, c.wqb_d, 4), ("wk", c.wk_d, c.wkb_d, 1),
                                  ("wv", c.wv_d, c.wvb_d, 1), ("wo", c.wo_d, c.wob_d, 4)):
            rows = src.shape[0] // nsl
            for i in range(nsl):
                S.dma(e, "gpsimd", "wcast", dst[i * rows:(i + 1) * rows, :],
                      src[i * rows:(i + 1) * rows, :],
                      f"p0.{nm}bf{i}" if i == nsl - 1 else None)

        if "t_wsum" in c.taps:
            S.wait(sy, "cc.wsum")
            S.tap_dma(e, c.taps["t_wsum"][:, :], c.wsum_d[:, :], "tap.wsum")

        def ln_stats(tag, src_dram, dep=None):
            for cc2 in range(CC):
                sl = cc2 % 2
                xb, xbk = c.ebs[sl], c.ebk[sl]
                qb, qbk = c.ebs[2 + sl], c.ebk[2 + sl]
                S.wr(gp, xbk)
                if dep is not None:
                    S.wait(gp, dep(cc2))
                mld = S.dma(e, "gpsimd", f"lnld{sl}", xb,
                            src_dram[cc2 * 128:(cc2 + 1) * 128, :], f"{tag}.ld{cc2}")
                S.wait(se, mld)
                S.wr(se, qbk)
                ins = se.activation(qb, xb, AF.Square)
                msq_ = S.step(ins, "se", 1, f"{tag}.sq{cc2}")
                if cc2 == 0:
                    S.wait(te, "consts")
                    S.wr(te, "psA")
                    S.wr(te, "psB")
                S.wait(te, msq_)
                mm512(e, S, c.psA, c.ones128b, xb,
                      start=(cc2 == 0), stop=(cc2 == CC - 1))
                mk = mm512(e, S, c.psB, c.ones128b, qb,
                           start=(cc2 == 0), stop=(cc2 == CC - 1), mark=f"{tag}.mm{cc2}")
                S.rd(xbk, mk)
                S.rd(qbk, mk)
            S.wait(se, f"{tag}.mm{CC - 1}")
            S.wr(se, "s_a")
            ins = se.activation(c.s_a, c.psA, AF.Copy, scale=1.0 / C)
            mm_ = S.step(ins, "se", 1, f"{tag}.m")
            S.rd("psA", mm_)
            S.wait(ve, mm_)
            S.wr(ve, "s_b")
            ins = ve.tensor_tensor(c.s_b, c.s_a, c.s_a, OP.mult)
            S.step(ins, "ve", 1)
            ins = ve.scalar_tensor_tensor(c.s_b, c.psB, 1.0 / C, c.s_b,
                                          OP.mult, OP.subtract)
            mv = S.step(ins, "ve", 1, f"{tag}.var")
            S.rd("psB", mv)
            S.wait(se, mv)
            ins = se.activation(c.s_b, c.s_b, AF.Ln, bias=c.eps5)
            S.step(ins, "se", 1)
            ins = se.activation(c.s_b, c.s_b, AF.Exp, scale=-0.5)
            mr = S.step(ins, "se", 1, f"{tag}.rsq")
            S.wait(ve, mr)
            S.wr(ve, "shiftb")
            ins = ve.scalar_tensor_tensor(c.shiftb, c.s_a[0:1, :], -1.0,
                                          c.s_b[0:1, :], OP.mult, OP.mult)
            ms_ = S.step(ins, "ve", 1, f"{tag}.shift")
            S.rd("s_a", ms_)

        def ln_apply(tag, src_dram, statag, dep=None):
            for cc2 in range(CC):
                sl = cc2 % 4
                S.wr(sy, f"xin{sl}")
                if dep is not None:
                    S.wait(sy, dep(cc2))
                mld = S.dma(e, "sync", f"xin{sl}", c.xin[sl],
                            src_dram[cc2 * 128:(cc2 + 1) * 128, :], f"{tag}.ld{cc2}")
                S.wait(ve, mld)
                if cc2 == 0:
                    S.wait(ve, f"{statag}.rsq")
                    S.wr(ve, "xt")
                ins = ve.tensor_tensor(c.xt(cc2), c.xin[sl], c.s_b, OP.mult)
                mk = S.step(ins, "ve", 1, f"{tag}.xt{cc2}")
                S.rd(f"xin{sl}", mk)
                S.rd("s_b", mk)

        def ffn(tag, xdram, outdram, atag, statag, dep=None, spill_eng="scalar"):
            if tag == "f1":
                S.wait(sy, "cc.wsum")
            for et in range(CC):
                sl = et % 4
                S.wr(sy, c.wbk[sl])
                mwld = S.dma(e, "sync", c.wbk[sl], c.wbs[sl],
                             c.wsum_d[:, et * 128:(et + 1) * 128]
                             .rearrange("(cc p) d -> p cc d", p=128), f"{tag}.wld{et}")
                xsl = et % 2
                S.wr(sy, f"xin{xsl}")
                if dep is not None:
                    S.wait(sy, dep(et))
                mxld = S.dma(e, "sync", f"xin{xsl}", c.xin[xsl],
                             xdram[et * 128:(et + 1) * 128, :], f"{tag}.xld{et}")
                ps = c.psA if et % 2 == 0 else c.psB
                psk = "psA" if et % 2 == 0 else "psB"
                S.wait(te, mwld)
                if et == 0:
                    if tag == "f1":
                        S.wait(te, "f1cs.col")
                    S.wait(te, f"{statag}.shift")
                S.wr(te, psk)
                for ccq in range(CC):
                    if et == 0:
                        S.wait(te, f"{atag}.xt{ccq}")
                    mm512(e, S, ps, c.wbc(sl, ccq), c.xt(ccq),
                          start=(ccq == 0), stop=False)
                mk = mm512(e, S, ps, c.wscol[:, et * 128:(et + 1) * 128], c.shiftb,
                           start=False, stop=True, mark=f"{tag}.mm{et}")
                S.rd(c.wbk[sl], mk)
                if et == CC - 1:
                    S.rd("xt", mk)
                    S.rd("shiftb", mk)
                S.wait(se, mk)
                S.wr(se, f"xin{2 + xsl}")
                ins = se.activation(c.xin[2 + xsl], ps, AF.Identity, scale=FFN_SCALE,
                                    bias=c.halfb[:, et:et + 1])
                mev = S.step(ins, "se", 1, f"{tag}.ev{et}")
                S.rd(psk, mev)
                S.wait(ve, mev)
                S.wait(ve, mxld)
                ins = ve.tensor_tensor(c.xin[xsl], c.xin[xsl], c.xin[2 + xsl], OP.add)
                ma = S.step(ins, "ve", 1, f"{tag}.add{et}")
                S.rd(f"xin{2 + xsl}", ma)
                S.wait(e[spill_eng], ma)
                msp = S.dma(e, spill_eng, f"sp{xsl}",
                            outdram[et * 128:(et + 1) * 128, :],
                            c.xin[xsl], f"{tag}.sp{et}")
                S.rd(f"xin{xsl}", msp)

        def colsum(tag, wd, outcol, n2, dep):
            S.wait(sy, dep)
            for cc2 in range(CC):
                sl = cc2 % 4
                S.wr(sy, c.wbk[sl])
                mld = S.dma(e, "sync", c.wbk[sl], c.wbs[sl][:, 0:n2],
                            wd[cc2 * 128:(cc2 + 1) * 128, :], f"{tag}.ld{cc2}")
                if cc2 == 0:
                    S.wait(te, "consts")
                    S.wr(te, "psA")
                S.wait(te, mld)
                mk = mm512(e, S, c.psA[0:1, :], c.onescol_b, c.wbs[sl][:, 0:n2],
                           start=(cc2 == 0), stop=(cc2 == CC - 1),
                           mark=f"{tag}.mm{cc2}", ncol=n2)
                S.rd(c.wbk[sl], mk)
            S.wait(se, f"{tag}.mm{CC - 1}")
            ins = se.activation(outcol, c.psA[0:1, 0:n2], AF.Copy)
            mw = S.step(ins, "se", 1, f"{tag}.col")
            S.rd("psA", mw)

        ln_stats("l1s", c.x_d)
        colsum("f1cs", c.wsum_d, c.wscol, C, "cc.wsum")
        if "t_rsq1" in c.taps:
            S.wait(sy, "l1s.rsq")
            S.tap_dma(e, c.taps["t_rsq1"][:, :], c.s_b, "tap.rsq1")
            S.rd("s_b", "tap.rsq1")
        ln_apply("a1", c.x_d, "l1s")
        ffn("f1", c.x_d, c.x1_d, "a1", "l1s")
        if "t_x1" in c.taps:
            S.wait(sy, "f1.sp15")
            S.tap_dma(e, c.taps["t_x1"][:, :], c.x1_d[:, :], "tap.x1")

        colsum("csq", c.wqb_d, c.wqcol, C, "p0.wobf3")
        colsum("csk", c.wkb_d, c.wkcol, 512, "p0.wobf3")
        colsum("csv", c.wvb_d, c.wvcol, 512, "p0.wobf3")
        ln_stats("l2s", c.x1_d, dep=lambda cc2: f"f1.sp{cc2}")
        ln_apply("a2", c.x1_d, "l2s")
        if "t_xt" in c.taps:
            for cc2 in range(CC):
                S.wait(sy, f"a2.xt{cc2}")
            m = S.tap_dma(e, c.taps["t_xt"][:, :], c.xt_big, "tap.xt")
            S.rd("xt", m)

        for g in range(NG):
            sl = g % 4
            st_ = g % 2
            pp, ppk = (c.psA, "psA") if g % 2 == 0 else (c.psB, "psB")
            mp, mpk = (c.psB, "psB") if g % 2 == 0 else (c.psA, "psA")
            wcol, wcs = c.wkcol, slice(g * 128, (g + 1) * 128)
            S.wr(sy, c.wbk[sl])
            mwld = S.dma(e, "sync", c.wbk[sl], c.wbs[sl][:, 0:T],
                         c.wkb_d[:, g * 128:(g + 1) * 128]
                         .rearrange("(cc p) d -> p cc d", p=128), f"k.wld{g}")
            S.wait(te, mwld)
            if g == 0:
                S.wait(te, "csv.col")
                for k_ in ("psA", "psB"):
                    S.wr(te, k_)
            mks = []
            for hs in range(2):
                hsl = slice(hs * 1024, (hs + 1) * 1024)
                S.wr(te, f"{ppk}{hs}")
                for ccq in range(CC):
                    if g == 0 and hs == 0:
                        S.wait(te, f"a2.xt{ccq}")
                    mm512(e, S, pp[:, hsl], c.wbc(sl, ccq), c.xt(ccq)[:, hsl],
                          start=(ccq == 0), stop=False, ncol=1024)
                mk = mm512(e, S, pp[:, hsl], wcol[:, wcs],
                           c.shiftb[:, hsl], start=False, stop=True,
                           mark=f"k.mm{g}_{hs}", ncol=1024)
                mks.append(mk)
            S.rd(c.wbk[sl], mks[1])
            sqm = []
            for hs in range(2):
                hsl = slice(hs * 1024, (hs + 1) * 1024)
                S.wait(se, mks[1])
                S.wr(se, "qsq")
                ins = se.activation(c.qsq[:, hsl], pp[:, hsl], AF.Square)
                sqm.append(S.step(ins, "se", 1, f"k.sq{g}_{hs}"))
            mqm = []
            for hs in range(2):
                hsl = slice(hs * 1024, (hs + 1) * 1024)
                S.wait(te, sqm[hs])
                S.wr(te, f"{mpk}{hs}")
                mq = mm512(e, S, mp[:, hsl], c.ones128b, c.qsq[:, hsl],
                           start=True, stop=True, mark=f"k.msq{g}_{hs}",
                           ncol=1024)
                S.rd("qsq", mq)
                mqm.append(mq)
            mrs = []
            for hs in range(2):
                hsl = slice(hs * 1024, (hs + 1) * 1024)
                S.wait(se, mqm[1])
                S.wr(se, "s_b")
                ins = se.activation(c.s_b[:, hsl], mp[:, hsl], AF.Ln,
                                    scale=1.0 / 128, bias=c.eps6)
                ml = S.step(ins, "se", 1, f"k.ln{g}_{hs}")
                S.rd(f"{mpk}{hs}", ml)
                ins = se.activation(c.s_b[:, hsl], c.s_b[:, hsl], AF.Exp,
                                    scale=-0.5)
                mrs.append(S.step(ins, "se", 1, f"k.rsq{g}_{hs}"))
            mevs = []
            for hs in range(2):
                hsl = slice(hs * 1024, (hs + 1) * 1024)
                S.wait(ve, mrs[hs])
                if hs == 0:
                    S.wr(ve, f"qh2{st_}")
                ins = ve.tensor_tensor(c.qh2[st_][:, hsl], pp[:, hsl],
                                       c.s_b[:, hsl], OP.mult)
                mev = S.step(ins, "ve", 1, f"k.ev{g}_{hs}")
                S.rd(f"{ppk}{hs}", mev)
                S.rd("s_b", mev)
                mevs.append(mev)
            S.wait(gp, mevs[1])
            mki = S.dma(e, "gpsimd", "kvw", c.kvk_in[g, :, :], c.qh2[st_], f"k.in{g}")
            S.rd(f"qh2{st_}", mki)

        for g in range(NG):
            S.wait(gp, f"k.in{g}")
        ins = gp.collective_compute(
            "AllGather", OP.bypass, ins=[c.kvk_in[:, :, :]], outs=[c.kvk_out[:, :, :, :]],
            replica_groups=[[i, i + 1] for i in range(0, NCORE, 2)])
        S.step(ins, "cc", 1, "cc.k0")

        wvv = c.wvb_d.rearrange("(cg cc p) d -> cg p cc d", cg=4, p=128)
        for ttg in range(4):
            tts = [4 * ttg + i for i in range(4)]
            for ccg in range(4):
                sl = (ttg * 4 + ccg) % 4
                S.wr(sy, c.wbk[sl])
                mwv = S.dma(e, "sync", c.wbk[sl],
                            c.wbs[sl].rearrange("p (q d) -> p q d", q=4),
                            wvv[ccg, :, :, :], f"v.wld{ttg}_{ccg}")
                S.wait(te, mwv)
                if ttg == 0 and ccg == 0:
                    S.wait(te, "csv.col")
                    for k_ in ("psB", "psB0", "psB1"):
                        S.wr(te, k_)
                if ccg == 0:
                    for tt in tts:
                        if tt >= 4:
                            S.wait(te, f"v.ev{tt - 4}")
                mk = None
                for cc2 in range(ccg * 4, (ccg + 1) * 4):
                    for tt in tts:
                        vp = c.psB[:, (tt % 4) * 512:(tt % 4 + 1) * 512]
                        ins = te.matmul(vp, c.xt(cc2)[:, tt * 128:(tt + 1) * 128],
                                        c.wbs[sl][:, (cc2 % 4) * 512:(cc2 % 4 + 1) * 512],
                                        start=(cc2 == 0), stop=False)
                mk = S.step(ins, "te", 1, f"v.wcons{ttg}_{ccg}")
                S.rd(c.wbk[sl], mk)
            for tt in tts:
                vp = c.psB[:, (tt % 4) * 512:(tt % 4 + 1) * 512]
                ins = te.matmul(vp, c.shiftb[:, tt * 128:(tt + 1) * 128], c.wvcol,
                                start=False, stop=True)
                mk = S.step(ins, "te", 1, f"v.mm{tt}")
                S.wait(se, mk)
                if tt == 0:
                    for i in range(4):
                        S.wr(se, c.ebk[i])
                vpv = vp.rearrange("p (g d) -> p g d", g=4)
                ins = se.activation(c.e3view[:, :, tt, :], vpv[:, 0:3, :], AF.Copy)
                S.step(ins, "se", 1)
                ins = se.activation(c.pairview[:, tt, :], vpv[:, 3, :], AF.Copy)
                S.step(ins, "se", 1, f"v.ev{tt}")
        S.rd("psB", f"v.ev{CC - 1}")
        for g in range(NG):
            S.wait(gp, f"v.ev{CC - 1}")
            src = c.ebs[g]
            mvi = S.dma(e, "gpsimd", "kvw", c.kvv_in[g, :, :], src, f"v.in{g}")
            S.rd(c.ebk[g], mvi)
        if "t_kvin" in c.taps:
            S.tap_dma(e, c.taps["t_kvin"][:, :, :], c.kvv_in[:, :, :], "tap.kvin")
            S.wait(sy, "tap.kvin")
        for g in range(NG):
            S.wait(gp, f"v.in{g}")
        ins = gp.collective_compute(
            "AllGather", OP.bypass, ins=[c.kvv_in[:, :, :]], outs=[c.kvv_out[:, :, :, :]],
            replica_groups=[[i, i + 1] for i in range(0, NCORE, 2)])
        S.step(ins, "cc", 1, "cc.v0")
        ins = gp.collective_compute(
            "AllReduce", OP.add, ins=[c.bar_in[:, :]], outs=[c.bar2_out[:, :]],
            replica_groups=[[i, i + 1] for i in range(0, NCORE, 2)])
        S.step(ins, "cc", 1, "cc.kv")

        for h in range(NH):
            sl = h % 4
            st_ = h % 2
            pp, ppk = (c.psA, "psA") if h % 2 == 0 else (c.psB, "psB")
            mp, mpk = (c.psB, "psB") if h % 2 == 0 else (c.psA, "psA")
            S.wr(sy, c.wbk[sl])
            mwld = S.dma(e, "sync", c.wbk[sl], c.wbs[sl][:, 0:T],
                         c.wqb_d[:, h * 128:(h + 1) * 128]
                         .rearrange("(cc p) d -> p cc d", p=128), f"q.wld{h}")
            S.wait(te, mwld)
            if h == 0:
                S.wait(te, "csq.col")
            mks = []
            for hs in range(2):
                hsl = slice(hs * 1024, (hs + 1) * 1024)
                S.wr(te, f"{ppk}{hs}")
                for ccq in range(CC):
                    mm512(e, S, pp[:, hsl], c.wbc(sl, ccq), c.xt(ccq)[:, hsl],
                          start=(ccq == 0), stop=False, ncol=1024)
                mk = mm512(e, S, pp[:, hsl], c.wqcol[:, h * 128:(h + 1) * 128],
                           c.shiftb[:, hsl], start=False, stop=True,
                           mark=f"q.mm{h}_{hs}", ncol=1024)
                mks.append(mk)
            S.rd(c.wbk[sl], mks[1])
            if h == NH - 1:
                S.rd("xt", mks[1])
                S.rd("shiftb", mks[1])
            sqm = []
            for hs in range(2):
                hsl = slice(hs * 1024, (hs + 1) * 1024)
                S.wait(se, mks[1])
                S.wr(se, "qsq")
                ins = se.activation(c.qsq[:, hsl], pp[:, hsl], AF.Square)
                sqm.append(S.step(ins, "se", 1, f"q.sq{h}_{hs}"))
            mqm = []
            for hs in range(2):
                hsl = slice(hs * 1024, (hs + 1) * 1024)
                S.wait(te, sqm[hs])
                S.wr(te, f"{mpk}{hs}")
                mq = mm512(e, S, mp[:, hsl], c.ones128b, c.qsq[:, hsl],
                           start=True, stop=True, mark=f"q.msq{h}_{hs}",
                           ncol=1024)
                S.rd("qsq", mq)
                mqm.append(mq)
            mrs = []
            for hs in range(2):
                hsl = slice(hs * 1024, (hs + 1) * 1024)
                S.wait(se, mqm[1])
                S.wr(se, "s_b")
                ins = se.activation(c.s_b[:, hsl], mp[:, hsl], AF.Ln,
                                    scale=1.0 / 128, bias=c.eps6)
                ml = S.step(ins, "se", 1, f"q.ln{h}_{hs}")
                S.rd(f"{mpk}{hs}", ml)
                ins = se.activation(c.s_b[:, hsl], c.s_b[:, hsl], AF.Exp,
                                    scale=-0.5, bias=c.qkb)
                mrs.append(S.step(ins, "se", 1, f"q.rsq{h}_{hs}"))
            mevs = []
            for hs in range(2):
                hsl = slice(hs * 1024, (hs + 1) * 1024)
                S.wait(ve, mrs[hs])
                if hs == 0:
                    S.wr(ve, f"qh2{st_}")
                ins = ve.tensor_tensor(c.qh2[st_][:, hsl], pp[:, hsl],
                                       c.s_b[:, hsl], OP.mult)
                mev = S.step(ins, "ve", 1, f"q.ev{h}_{hs}")
                S.rd(f"{ppk}{hs}", mev)
                S.rd("s_b", mev)
                mevs.append(mev)
            S.wait(sy, mevs[1])
            msp = S.dma(e, "sync", f"qsp{st_}", c.qh_d[h, :, :], c.qh2[st_], f"q.sp{h}")
            S.rd(f"qh2{st_}", msp)
        if "t_qh" in c.taps:
            S.wait(sy, f"q.sp{NH - 1}")
            S.tap_dma(e, c.taps["t_qh"][:, :, :], c.qh_d[:, :, :], "tap.qh")
        if "t_kvout" in c.taps:
            S.wait(sy, "cc.kv")
            S.tap_dma(e, c.taps["t_kvout"][:, :, :, :], c.kvv_out[:, :, :, :],
                      "tap.kvout")

        for g in range(NG):
            S.wr(sy, "khat")
            if g == 0:
                S.wait(sy, "cc.kv")
            mk0 = S.dma(e, "sync", "khA", c.khat[:, 0:T], c.kvk_out[0, g, :, :],
                        f"at.k0{g}")
            mk1 = S.dma(e, "sync", "khB", c.khat[:, T:], c.kvk_out[1, g, :, :],
                        f"at.k1{g}")
            S.wr(sy, "vg")
            mv0 = S.dma(e, "sync", "vg0", c.vg[:, 0:T], c.kvv_out[0, g, :, :],
                        f"at.v0{g}")
            mv1 = S.dma(e, "sync", "vg1", c.vg[:, T:], c.kvv_out[1, g, :, :],
                        f"at.v1{g}")
            for h4 in range(4):
                h = 4 * g + h4
                sl = h % 2
                S.wr(sy, f"qh2{sl}")
                S.wait(sy, f"q.sp{h}")
                mq = S.dma(e, "sync", f"qld{sl}", c.qh2[sl], c.qh_d[h, :, :],
                           f"at.q{h}")
                if g == 0 and h4 == 0:
                    for k_ in ("psA", "psB"):
                        S.wr(te, k_)
                    S.wait(te, "consts2")
                for qs in range(2):
                    qsl = slice(qs * 1024, (qs + 1) * 1024)
                    sbufs = [c.psA[:, 0:1024], c.psA[:, 1024:2048]]
                    ob = c.psB[:, qs * 1024:(qs + 1) * 1024]
                    for kt in range(NKT):
                        sk = f"psA{kt % 2}"
                        esl = kt % 3
                        if kt == 0 and qs == 0:
                            S.wait(te, mq)
                            if h4 == 0:
                                S.wait(te, mk0)
                                S.wait(te, mk1)
                                S.wait(te, mv0)
                                S.wait(te, mv1)
                        S.wr(te, sk)
                        msc = mm512(e, S, sbufs[kt % 2],
                                    c.khat[:, kt * 128:(kt + 1) * 128],
                                    c.qh2[sl][:, qsl], start=True, stop=True,
                                    mark=f"at.sc{h}_{qs}_{kt}")
                        if kt == NKT - 1:
                            S.rd(f"qh2{sl}", msc)
                            if h4 == 3 and qs == 1:
                                S.rd("khat", msc)
                        if kt > 0:
                            if kt == 1:
                                S.wr(te, f"psB{qs}")
                            mav = mm512(e, S, ob, c.vg[:, (kt - 1) * 128:kt * 128],
                                        c.e6[(kt - 1) % 3],
                                        start=(kt == 1), stop=False,
                                        mark=f"at.av{h}_{qs}_{kt - 1}", ncol=1024)
                            S.rd(c.ebk[(kt - 1) % 3], mav)
                        S.wait(se, msc)
                        S.wr(se, c.ebk[esl])
                        ins = se.activation(c.e6[esl], sbufs[kt % 2], AF.Exp)
                        mex = S.step(ins, "se", 1, f"at.ex{h}_{qs}_{kt}")
                        S.rd(sk, mex)
                        if kt % 2 == 1:
                            S.wait(ve, mex)
                            if QUADTREE:
                                dst = c.pair[:, qsl] if kt % 4 == 1 else c.qsq[:, qsl]
                                dk = "ebp" if kt % 4 == 1 else "qsq"
                            else:
                                dst, dk = c.pair[:, qsl], "ebp"
                            S.wr(ve, dk)
                            ins = ve.tensor_tensor(dst, c.e6[(kt - 1) % 3],
                                                   c.e6[esl], OP.add)
                            mp = S.step(ins, "ve", 1, f"at.pr{h}_{qs}_{kt}")
                            S.rd(c.ebk[(kt - 1) % 3], mp)
                            S.rd(c.ebk[esl], mp)
                            if QUADTREE and kt % 4 == 3:
                                ins = ve.tensor_tensor(c.qsq[:, qsl], c.pair[:, qsl],
                                                       c.qsq[:, qsl], OP.add)
                                mp2 = S.step(ins, "ve", 1, f"at.qd{h}_{qs}_{kt}")
                                S.rd("ebp", mp2)
                                if kt == 3:
                                    S.wr(ve, f"s_a{qs}")
                                    ins = ve.tensor_copy(c.s_a[:, qsl], c.qsq[:, qsl])
                                else:
                                    ins = ve.tensor_tensor(c.s_a[:, qsl], c.s_a[:, qsl],
                                                           c.qsq[:, qsl], OP.add)
                                mac = S.step(ins, "ve", 1, f"at.ac{h}_{qs}_{kt}")
                                S.rd("qsq", mac)
                            elif not QUADTREE:
                                if kt == 1:
                                    S.wr(ve, f"s_a{qs}")
                                    ins = ve.tensor_copy(c.s_a[:, qsl], c.pair[:, qsl])
                                else:
                                    ins = ve.tensor_tensor(c.s_a[:, qsl], c.s_a[:, qsl],
                                                           c.pair[:, qsl], OP.add)
                                mac = S.step(ins, "ve", 1, f"at.ac{h}_{qs}_{kt}")
                                S.rd("ebp", mac)
                    S.wait(te, f"at.ex{h}_{qs}_{NKT - 1}")
                    mav = mm512(e, S, ob, c.vg[:, (NKT - 1) * 128:],
                                c.e6[(NKT - 1) % 3],
                                start=False, stop=True,
                                mark=f"at.av{h}_{qs}_{NKT - 1}", ncol=1024)
                    S.rd(c.ebk[(NKT - 1) % 3], mav)
                    if h4 == 3 and qs == 1:
                        S.rd("vg", mav)
                    S.wait(te, f"at.ac{h}_{qs}_{NKT - 1}")
                    S.wr(te, "psA0")
                    S.wr(te, "psA1")
                    md = mm512(e, S, c.psA[:, 0:1024], c.ones128, c.s_a[:, qsl],
                               start=True, stop=True, mark=f"at.ds{h}_{qs}",
                               ncol=1024)
                    S.rd(f"s_a{qs}", md)
                    S.wait(se, md)
                    S.wr(se, "s_b")
                    ins = se.activation(c.s_b[:, 0:1024], c.psA[:, 0:1024], AF.Ln)
                    ml_ = S.step(ins, "se", 1, f"at.dln{h}_{qs}")
                    S.rd("psA0", ml_)
                    S.wr(se, f"arcp{qs}")
                    ins = se.activation(c.arcp[:, qsl], c.s_b[:, 0:1024], AF.Exp,
                                        scale=-1.0)
                    mrc = S.step(ins, "se", 1, f"at.rc{h}_{qs}")
                    S.rd("s_b", mrc)
                    S.wait(ve, mav)
                    S.wait(ve, mrc)
                    if h == 0 and qs == 0:
                        S.wr(ve, "xt")
                    ins = ve.tensor_tensor(c.xt(h)[:, qsl], ob, c.arcp[:, qsl],
                                           OP.mult)
                    mo = S.step(ins, "ve", 1, f"at.o{h}_{qs}")
                    S.rd(f"psB{qs}", mo)
                    S.rd(f"arcp{qs}", mo)
        if "t_osc" in c.taps:
            for h in range(NH):
                S.wait(sy, f"at.o{h}_0")
                S.wait(sy, f"at.o{h}_1")
            m = S.tap_dma(e, c.taps["t_osc"][:, :], c.xt_big, "tap.osc")
            S.rd("xt", m)

        S.wait(sy, "p0.wobf3")
        for ct in range(CC):
            sl = ct % 4
            xsl = ct % 2
            S.wr(sy, c.wbk[sl])
            mwld = S.dma(e, "sync", c.wbk[sl], c.wbs[sl],
                         c.wob_d[:, ct * 128:(ct + 1) * 128]
                         .rearrange("(hh p) d -> p hh d", p=128), f"wo.wld{ct}")
            S.wr(sy, f"xin{xsl}")
            S.wait(sy, f"f1.sp{ct}")
            mxld = S.dma(e, "sync", f"xin{xsl}", c.xin[xsl],
                         c.x1_d[ct * 128:(ct + 1) * 128, :], f"wo.xld{ct}")
            ps = c.psA if ct % 2 == 0 else c.psB
            psk = "psA" if ct % 2 == 0 else "psB"
            S.wait(te, mwld)
            if ct == 0:
                for k_ in ("psA0", "psA1", "psB0", "psB1"):
                    S.wr(te, k_)
                for h in range(NH):
                    S.wait(te, f"at.o{h}_0")
                    S.wait(te, f"at.o{h}_1")
            S.wr(te, psk)
            mk = None
            for h in range(NH):
                mk = mm512(e, S, ps, c.wbc(sl, h), c.xt(h),
                           start=(h == 0), stop=(h == NH - 1),
                           mark=f"wo.mm{ct}" if h == NH - 1 else None)
            S.rd(c.wbk[sl], mk)
            if ct == CC - 1:
                S.rd("xt", mk)
            S.wait(ve, mk)
            S.wait(ve, mxld)
            ins = ve.scalar_tensor_tensor(c.xin[xsl], ps, 1.0, c.xin[xsl],
                                          OP.mult, OP.add)
            ma = S.step(ins, "ve", 1, f"wo.add{ct}")
            S.rd(psk, ma)
            S.wait(se, ma)
            msp = S.dma(e, "scalar", f"sp{xsl}", c.x2_d[ct * 128:(ct + 1) * 128, :],
                        c.xin[xsl], f"wo.sp{ct}")
            S.rd(f"xin{xsl}", msp)
        if "t_x2" in c.taps:
            S.wait(sy, f"wo.sp{CC - 1}")
            S.tap_dma(e, c.taps["t_x2"][:, :], c.x2_d[:, :], "tap.x2")

        ln_stats("l3s", c.x2_d, dep=lambda cc2: f"wo.sp{cc2}")
        ln_apply("a3", c.x2_d, "l3s")
        ffn("f3", c.x2_d, c.out_d, "a3", "l3s")

        totals = dict(S.counts)
        for en in ("tensor", "scalar", "vector", "gpsimd"):
            ins = e[en].nop()
            S.step(ins, "fin", 1)
        if not isinstance(sy, Fake):
            for name in sorted(totals):
                if totals[name] > 0:
                    sy.wait_ge(S.sems[name], totals[name])
            sy.wait_ge(S.sems["fin"], 4)
            for name in sorted(S.counts):
                sy.sem_clear(S.sems[name])

    fake = {n: Fake() for n in ENGINES}
    program(fake)
    S.frozen = True
    final_counts = dict(S.counts)

    from contextlib import ExitStack
    with ExitStack() as stack:
        sems = {}
        for name in sorted(final_counts):
            sems[name] = stack.enter_context(nc.semaphore(name))
        block = stack.enter_context(nc.Block())
        S.sems = sems

        def section(name):
            def f(eng):
                S.reset()
                program({**fake, name: eng})
                assert S.counts == final_counts, (name, S.counts, final_counts)
            return f

        block.sync(section("sync"))
        block.tensor(section("tensor"))
        block.scalar(section("scalar"))
        block.vector(section("vector"))
        block.gpsimd(section("gpsimd"))

    return nc


_CACHE = {}


def _get_nc():
    if "nc" not in _CACHE:
        _CACHE["nc"] = build_program(DEBUG_TAPS)
    return _CACHE["nc"]


def kernel(**inputs):
    x = np.asarray(inputs["L0_attn_x_q"], dtype=np.float32)
    wq = np.asarray(inputs["L0_attn_Wq"], dtype=np.float32).reshape(C, NH * HD)
    wk = np.asarray(inputs["L0_attn_Wk"], dtype=np.float32).reshape(C, NG * HD)
    wv = np.asarray(inputs["L0_attn_Wv"], dtype=np.float32).reshape(C, NG * HD)
    wo = np.ascontiguousarray(np.asarray(inputs["L0_attn_Wo"], dtype=np.float32))
    fac = np.asarray(inputs["L0_ffn_factors"], dtype=np.float32)
    bias = np.ascontiguousarray(
        np.asarray(inputs["L0_ffn_bias"], dtype=np.float32).reshape(CC, 128).T)

    xf = x.reshape(NCORE * T, C)
    nc = _get_nc()
    in_maps = []
    for cid in range(NCORE):
        in_maps.append({
            "x": np.ascontiguousarray(xf[cid * T:(cid + 1) * T].T),
            "wq": wq, "wk": wk, "wv": wv, "wo": wo,
            "fac": np.ascontiguousarray(fac[:, cid * 256:(cid + 1) * 256, :]),
            "bias": bias,
        })
    res = run_bass_kernel_spmd(nc, in_maps, core_ids=list(range(NCORE)))
    out = np.empty((NCORE * T, C), dtype=np.float32)
    for cid in range(NCORE):
        out[cid * T:(cid + 1) * T] = res.results[cid]["out"].T
    return out.reshape(4, 4096, C)
